# version 03 (final)
# baseline (speedup 1.0000x reference)
import numpy as np
import ml_dtypes
import concourse.bass as bass
import concourse.mybir as mybir
from concourse.bass_utils import run_bass_kernel_spmd

F32 = mybir.dt.float32
BF16 = mybir.dt.bfloat16
AF = mybir.ActivationFunctionType
ALU = mybir.AluOpType
AX = mybir.AxisListType
BF = ml_dtypes.bfloat16

D = 2048
T = 8192
NH = 16
HID = 5632
OWN = 1024
NC = 8
KB = 1024
QSC = 0.08838834764831843
NW = HID * D
EPS_LN = 1e-5
EPS_Q = 1e-8
EXP_SHIFT = -4.0

SEM_DMA = ("ld", "st", "d0", "d1", "d2", "d3", "g0", "g1", "g2", "g3")
DEBUG = False
LAST_RES = None


class Builder:
    def __init__(self):
        self.ops = {"sync": [], "gpsimd": [], "tensor": [], "vector": [], "scalar": []}
        self.cnt = {s: 0 for s in SEM_DMA} | {"tm": 0, "vm": 0, "sm": 0}

    def op(self, eng, fn, waits=(), inc=None):
        w = {}
        for tok in waits:
            if tok is None:
                continue
            s, v = tok
            if v > 0:
                w[s] = max(w.get(s, 0), v)
        for s, v in w.items():
            assert v <= self.cnt[s], f"wait on {s} for {v} but only {self.cnt[s]} emitted"
        tok = None
        if inc is not None:
            amt = 16 if inc in SEM_DMA else 1
            self.cnt[inc] += amt
            tok = (inc, self.cnt[inc])
        self.ops[eng].append((sorted(w.items()), fn, inc))
        return tok


def build_nc():
    nc = bass.Bass("TRN2", target_bir_lowering=False)

    xT = nc.dram_tensor("xT", [D, T], F32, kind="ExternalInput").ap()
    xownT = nc.dram_tensor("xownT", [D, OWN], F32, kind="ExternalInput").ap()
    wq = nc.dram_tensor("wq", [D, D], F32, kind="ExternalInput").ap()
    wk = nc.dram_tensor("wk", [D, 256], F32, kind="ExternalInput").ap()
    wv = nc.dram_tensor("wv", [D, 256], F32, kind="ExternalInput").ap()
    wo = nc.dram_tensor("wo", [D, D], F32, kind="ExternalInput").ap()
    bqv = nc.dram_tensor("bq", [D], F32, kind="ExternalInput").ap()
    bkv = nc.dram_tensor("bk", [256], F32, kind="ExternalInput").ap()
    bvv = nc.dram_tensor("bv", [256], F32, kind="ExternalInput").ap()
    bov = nc.dram_tensor("bo", [D], F32, kind="ExternalInput").ap()
    w1 = nc.dram_tensor("w1", [HID, D], F32, kind="ExternalInput").ap()
    w2 = nc.dram_tensor("w2", [HID, D], F32, kind="ExternalInput").ap()
    w3 = nc.dram_tensor("w3", [D, HID], F32, kind="ExternalInput").ap()
    qrow = nc.dram_tensor("qrow", [1, OWN], F32, kind="ExternalInput").ap()
    sconst_in = nc.dram_tensor("sconst", [128, 2], F32, kind="ExternalInput").ap()
    iota_in = nc.dram_tensor("iota", [128, 1], F32, kind="ExternalInput").ap()
    ones_bf_in = nc.dram_tensor("ones_bf", [128, 128], BF16, kind="ExternalInput").ap()
    ones_f_in = nc.dram_tensor("ones_f", [128, 128], F32, kind="ExternalInput").ap()
    ident_in = nc.dram_tensor("ident", [128, 128], F32, kind="ExternalInput").ap()
    outT = nc.dram_tensor("outT", [D, OWN], F32, kind="ExternalOutput").ap()
    x1s = nc.dram_tensor("x1s", [D, OWN], F32).ap()
    fs = nc.dram_tensor("fs", [D, OWN], BF16).ap()
    w3qts = nc.dram_tensor("w3qts", [44 * 128, D], BF16).ap()
    dbg = {}
    if DEBUG:
        for nm in ("dbg_xn", "dbg_q", "dbg_k", "dbg_v", "dbg_acat", "dbg_x1", "dbg_x2n",
                   "dbg_hA", "dbg_hB", "dbg_hC"):
            dbg[nm] = nc.dram_tensor(nm, [128, 16384], BF16, kind="ExternalOutput").ap()
        dbg["dbg_gth"] = nc.dram_tensor("dbg_gth", [128, 8], F32, kind="ExternalOutput").ap()
        dbg["dbg_fs"] = nc.dram_tensor("dbg_fs", [D, OWN], BF16, kind="ExternalOutput").ap()

    B = Builder()
    import contextlib

    ctx = contextlib.ExitStack()
    with ctx:
        sems = {n: ctx.enter_context(nc.semaphore(n)) for n in B.cnt}

        BASE = 16512

        def view(name, shape, dtype, off):
            return nc.alloc_sbuf_tensor_at(name, shape, dtype, offset=BASE + off)

        ones_bf = view("v_ones_bf", [128, 128], BF16, 0)
        ones_f = view("v_ones_f", [128, 128], F32, 256)
        identf = view("v_identf", [128, 128], F32, 768)
        bq_sb = view("v_bq", [128, 16], F32, 1280)
        bo_sb = view("v_bo", [128, 16], F32, 1344)
        bk_sb = view("v_bk", [128, 2], F32, 1408)
        iota_sb = view("v_iota", [128, 1], F32, 1440)
        gacc = view("v_gacc", [128, 64], F32, 1472)
        gth = view("v_gth", [128, 8], F32, 1728)
        sconst = view("v_sconst", [128, 2], F32, 1760)
        bv_bc = view("v_bvbc", [128, 256], F32, 1792)
        rb_sb = view("v_rb", [128, 256], F32, 2816)
        m_sb = view("v_m", [128, 512], F32, 4 * KB)
        t1_sb = view("v_t1", [128, 512], F32, 6 * KB)
        r_sb = view("v_r", [128, 512], F32, 8 * KB)
        mr_sb = view("v_mr", [128, 512], F32, 10 * KB)
        r_bf = view("v_rbf", [128, 512], BF16, 12 * KB)
        mr_bf = view("v_mrbf", [128, 512], BF16, 13 * KB)
        xsq = view("v_xsq", [128, 512], BF16, 14 * KB)
        tscr = view("v_tscr", [128, 512], BF16, 15 * KB)
        xt0 = view("v_xt0", [128, 512], F32, 16 * KB)
        xt1 = view("v_xt1", [128, 512], F32, 18 * KB)
        exp0 = view("v_exp0", [128, 512], BF16, 16 * KB)
        exp1 = view("v_exp1", [128, 512], BF16, 17 * KB)
        fst0 = view("v_fst0", [128, 512], BF16, 18 * KB)
        fst1 = view("v_fst1", [128, 512], BF16, 19 * KB)
        wkst = [view(f"v_wkst{i}", [128, 256], F32, 20 * KB + i * KB) for i in range(2)]
        wsl = [view(f"v_wsl{i}", [128, 2048], F32, 22 * KB + i * 8 * KB) for i in range(2)]
        wbf = [view(f"v_wbf{i}", [128, 2048], BF16, 38 * KB + i * 4 * KB) for i in range(2)]
        wkbf = view("v_wkbf", [128, 4096], BF16, 46 * KB)
        wvbf = view("v_wvbf", [128, 4096], BF16, 54 * KB)
        mask_sb = view("v_mask", [128, 4096], BF16, 46 * KB)
        qmi = view("v_qmi", [128, 1024], F32, 54 * KB)
        qrow_sb = view("v_qrow", [1, 1024], F32, 58 * KB)
        tq1 = [view(f"v_tq1{i}", [128, 2048], BF16, 46 * KB + i * 4 * KB) for i in range(2)]
        tq2 = [view(f"v_tq2{i}", [128, 2048], BF16, 54 * KB + i * 4 * KB) for i in range(2)]
        w3sl = [view(f"v_w3sl{i}", [128, 2048], BF16, 46 * KB + i * 4 * KB) for i in range(4)]
        BIG1, BIG2, BIG3, BIG4 = 62 * KB, 94 * KB, 126 * KB, 158 * KB
        kbf = view("v_kbf", [128, 16384], BF16, BIG1)
        x1bf = view("v_x1bf", [128, 16384], BF16, BIG1)
        vbf = view("v_vbf", [128, 16384], BF16, BIG2)
        x2n = view("v_x2n", [128, 16384], BF16, BIG2)
        xnsb = view("v_xnsb", [128, 8192], BF16, BIG3)
        xnown = view("v_xnown", [128, 16384], BF16, BIG3)
        acat = view("v_acat", [128, 16384], BF16, BIG3)
        qbf = view("v_qbf", [128, 16384], BF16, BIG4)
        aobf = view("v_aobf", [128, 16384], BF16, BIG4)
        hidA = view("v_hidA", [128, 16384], BF16, BIG1)
        hidB = view("v_hidB", [128, 16384], BF16, BIG3)
        hidC = view("v_hidC", [128, 16384], BF16, BIG4)

        def hid_slot(hc, qb):
            buf = hidA if hc < 16 else (hidB if hc < 32 else hidC)
            base = (hc % 16) * 1024 + qb * 512
            return buf[:, base:base + 512]

        pb = [ctx.enter_context(nc.psum_tensor(f"pb{i}", [128, 512], F32)) for i in range(8)]
        bank_free = [None] * 8

        op = B.op

        def dma(queue, dst, src, waits=(), slow=False, sem=None):
            if sem is None:
                sem = "ld" if queue == "sync" else "st"
            return op(queue,
                      lambda e, d=dst, s=src, sl=slow: e.dma_start(d, s, allow_slow_non_contiguous=sl),
                      waits, sem)

        def mmop(bank, out_ap, lhsT, rhs, start, stop, waits=(), inc=False):
            w = list(waits)
            if start and bank_free[bank] is not None:
                w.append(bank_free[bank])
            return op("tensor",
                      lambda e, o=out_ap, l=lhsT, r=rhs, st=start, sp=stop: e.matmul(o, l, r, start=st, stop=sp),
                      w, "tm" if inc else None)

        dma("sync", ones_bf[:, :], ones_bf_in[:, :])
        t_c2 = dma("sync", ones_f[:, :], ones_f_in[:, :])
        dma("sync", identf[:, :], ident_in[:, :])
        dma("sync", bass.AP(bq_sb, 0, [[16, 128], [1, 16]]),
            bass.AP(bqv.tensor, 0, [[1, 128], [128, 16]]), slow=True)
        dma("sync", bass.AP(bo_sb, 0, [[16, 128], [1, 16]]),
            bass.AP(bov.tensor, 0, [[1, 128], [128, 16]]), slow=True)
        dma("sync", bass.AP(bk_sb, 0, [[2, 128], [1, 2]]),
            bass.AP(bkv.tensor, 0, [[1, 128], [128, 2]]), slow=True)
        t_c7 = dma("sync", iota_sb[:, :], iota_in[:, :])
        t_c8 = dma("sync", m_sb[0:1, 0:256], bass.AP(bvv.tensor, 0, [[256, 1], [1, 256]]))
        t_sc = dma("sync", sconst[:, :], sconst_in[:, :])
        tb = mmop(7, pb[7][:, 0:256], ones_f[0:1, 0:128], m_sb[0:1, 0:256], True, True,
                  waits=[t_c2, t_c8], inc=True)
        t_bv = op("vector", lambda e: e.tensor_copy(bv_bc[:, :], pb[7][:, 0:256]), [tb], "vm")
        bank_free[7] = t_bv

        cast_last = None
        st_busy = [None, None]
        for ic in range(16):
            for j, (src, dstb) in enumerate(((wk, wkbf), (wv, wvbf))):
                sl = (2 * ic + j) % 2
                st = wkst[sl]
                w = [st_busy[sl]] if st_busy[sl] else []
                tld = dma("sync", st[:, :], src[ic * 128:(ic + 1) * 128, 0:256], waits=w,
                          sem="d2" if sl == 0 else "d3")
                tcst = op("vector",
                          lambda e, d=dstb, i=ic, s=st: e.tensor_copy(d[:, i * 256:(i + 1) * 256], s[:, :]),
                          [tld], "vm")
                st_busy[sl] = tcst
        wkv_done = cast_last = ("vm", B.cnt["vm"])

        sA, sB = 4, 5

        def ln_stats_mm(src_bf, sq_ap, first, last, waits=()):
            mmop(sA, pb[sA][:, 0:512], ones_bf[:, :], src_bf, first, last,
                 waits=waits, inc=last)
            ta = ("tm", B.cnt["tm"]) if last else None
            tbq = mmop(sB, pb[sB][:, 0:512], ones_bf[:, :], sq_ap, first, last,
                       waits=list(waits), inc=True)
            return ta, tbq

        def ln_finalize(waits):
            t1 = op("vector", lambda e: e.tensor_scalar(m_sb[:, :], pb[sA][:, 0:512], 1.0 / D, None, ALU.mult),
                    waits, "vm")
            t2 = op("vector", lambda e: e.tensor_mul(t1_sb[:, :], m_sb[:, :], m_sb[:, :]), [t1], "vm")
            t3 = op("vector",
                    lambda e: e.scalar_tensor_tensor(t1_sb[:, :], pb[sB][:, 0:512], 1.0 / D, t1_sb[:, :],
                                                     ALU.mult, ALU.subtract),
                    [t2], "vm")
            bank_free[sA] = t3
            bank_free[sB] = t3
            t4 = op("scalar", lambda e: e.activation(t1_sb[:, :], t1_sb[:, :], AF.Ln, bias=sconst[:, 0:1]),
                    [t3, t_sc], "sm")
            t5 = op("scalar", lambda e: e.activation(r_sb[:, :], t1_sb[:, :], AF.Exp, scale=sconst[:, 1:2]),
                    [t4], "sm")
            t6 = op("vector", lambda e: e.tensor_copy(r_bf[:, :], r_sb[:, :]), [t5], "vm")
            t7 = op("vector", lambda e: e.tensor_mul(mr_sb[:, :], m_sb[:, :], r_sb[:, :]), [t5], "vm")
            t8 = op("vector", lambda e: e.tensor_copy(mr_bf[:, :], mr_sb[:, :]), [t7], "vm")
            return [t6, t8]

        def ln_apply(dst, src_bf, waits):
            ta = op("vector", lambda e, s=src_bf: e.tensor_mul(tscr[:, :], s, r_bf[:, :]), waits, "vm")
            return op("vector", lambda e, d=dst: e.tensor_sub(d, tscr[:, :], mr_bf[:, :]), [ta], "vm")

        kv_banks = [0, 1]
        v_banks = [2, 3]
        xt_busy = [None, None]
        prev_tbq = None
        phase_tok = None
        for sb in range(16):
            stat_tok = None
            for dc in range(16):
                sl = dc % 2
                xt = xt0 if sl == 0 else xt1
                w = [xt_busy[sl]] if xt_busy[sl] else []
                tld = dma("sync", xt[:, :], xT[dc * 128:(dc + 1) * 128, sb * 512:(sb + 1) * 512], waits=w,
                          sem="d0" if sl == 0 else "d1")
                slot = xnsb[:, dc * 512:(dc + 1) * 512]
                tc = op("vector", lambda e, s=slot, x=xt: e.tensor_copy(s, x[:, :]), [tld], "vm")
                tsq = op("scalar", lambda e, x=xt: e.activation(xsq[:, :], x[:, :], AF.Square),
                         [tld] + ([prev_tbq] if prev_tbq else []), "sm")
                sw = [tc, tsq]
                if dc == 0:
                    sw += [bank_free[sA], bank_free[sB]]
                ta, tbq = ln_stats_mm(slot, xsq[:, :], dc == 0, dc == 15, waits=sw)
                xt_busy[sl] = tbq
                prev_tbq = tbq
                if dc == 15:
                    stat_tok = [ta, tbq]
            fin = ln_finalize(stat_tok)
            app = fin
            for dc in range(16):
                slot = xnsb[:, dc * 512:(dc + 1) * 512]
                app = [ln_apply(slot, slot, app)]
            app_done = app[0]
            for kv in range(2):
                bnk = kv_banks[kv]
                for ic in range(16):
                    mmop(bnk, pb[bnk][:, 0:512], wkbf[:, ic * 256 + kv * 128:ic * 256 + kv * 128 + 128],
                         xnsb[:, ic * 512:(ic + 1) * 512], ic == 0, ic == 15,
                         waits=([app_done, wkv_done] if ic == 0 else []), inc=(ic == 15))
                tlast = ("tm", B.cnt["tm"])
                te = op("vector",
                        lambda e, b=bnk, k=kv, s=sb: e.tensor_scalar(
                            kbf[:, k * 8192 + s * 512:k * 8192 + (s + 1) * 512], pb[b][:, 0:512],
                            bk_sb[:, k:k + 1], None, ALU.add),
                        [tlast], "vm")
                bank_free[bnk] = te
            for tt in range(4):
                bnk = v_banks[tt % 2]
                for ic in range(16):
                    mmop(bnk, pb[bnk][:, 0:256], xnsb[:, ic * 512 + tt * 128:ic * 512 + tt * 128 + 128],
                         wvbf[:, ic * 256:(ic + 1) * 256], ic == 0, ic == 15,
                         waits=([app_done] if ic == 0 else []), inc=(ic == 15))
                tlast = ("tm", B.cnt["tm"])
                sc_g = sb * 4 + tt
                te = op("vector",
                        lambda e, b=bnk, s=sc_g: e.tensor_add(
                            vbf[:, s * 256:(s + 1) * 256], pb[b][:, 0:256], bv_bc[:, :]),
                        [tlast], "vm")
                bank_free[bnk] = te
            phase_tok = te

        for qb in range(2):
            stat_tok = None
            for dc in range(16):
                sl = dc % 2
                xt = xt0 if sl == 0 else xt1
                w = [xt_busy[sl]] if xt_busy[sl] else []
                tld = dma("sync", xt[:, :], xownT[dc * 128:(dc + 1) * 128, qb * 512:(qb + 1) * 512], waits=w,
                          sem="d0" if sl == 0 else "d1")
                slot = xnown[:, dc * 1024 + qb * 512:dc * 1024 + qb * 512 + 512]
                tc = op("vector", lambda e, s=slot, x=xt: e.tensor_copy(s, x[:, :]),
                        [tld] + ([phase_tok] if (qb == 0 and dc < 2) else []), "vm")
                tsq = op("scalar", lambda e, x=xt: e.activation(xsq[:, :], x[:, :], AF.Square),
                         [tld] + ([prev_tbq] if prev_tbq else []), "sm")
                sw = [tc, tsq]
                if dc == 0:
                    sw += [bank_free[sA], bank_free[sB]]
                ta, tbq = ln_stats_mm(slot, xsq[:, :], dc == 0, dc == 15, waits=sw)
                xt_busy[sl] = tbq
                prev_tbq = tbq
                if dc == 15:
                    stat_tok = [ta, tbq]
            fin = ln_finalize(stat_tok)
            app = fin
            for dc in range(16):
                slot = xnown[:, dc * 1024 + qb * 512:dc * 1024 + qb * 512 + 512]
                app = [ln_apply(slot, slot, app)]
        xnown_done = app[0]
        if DEBUG:
            dma("gpsimd", dbg["dbg_k"][:, :], kbf[:, :], waits=[phase_tok])
            dma("gpsimd", dbg["dbg_v"][:, :], vbf[:, :], waits=[phase_tok])
            dma("gpsimd", dbg["dbg_xn"][:, :], xnown[:, :], waits=[xnown_done])

        q_banks = [0, 1]
        wsl_busy = [None, None]
        wbf_busy = [None, None]
        for oc in range(16):
            sl = oc % 2
            w = [wbf_busy[sl]] if wbf_busy[sl] else []
            tld = dma("sync", bass.AP(wsl[sl], 0, [[2048, 128], [128, 16], [1, 128]]),
                      bass.AP(wq.tensor, oc * 128, [[2048, 128], [2048 * 128, 16], [1, 128]]),
                      waits=w, sem="d2" if sl == 0 else "d3")
            tc = op("vector", lambda e, s=sl: e.tensor_copy(wbf[s][:, :], wsl[s][:, :]),
                    [tld] + ([wsl_busy[sl]] if wsl_busy[sl] else []), "vm")
            wbf_busy[sl] = tc
            for qb in range(2):
                bnk = q_banks[(oc * 2 + qb) % 2]
                for ic in range(16):
                    mmop(bnk, pb[bnk][:, 0:512], wbf[sl][:, ic * 128:(ic + 1) * 128],
                         xnown[:, ic * 1024 + qb * 512:ic * 1024 + qb * 512 + 512], ic == 0, ic == 15,
                         waits=([tc, xnown_done] if ic == 0 else []), inc=(ic == 15))
                tlast = ("tm", B.cnt["tm"])
                te = op("vector",
                        lambda e, b=bnk, o=oc, q=qb: e.tensor_scalar(
                            qbf[:, o * 1024 + q * 512:o * 1024 + q * 512 + 512], pb[b][:, 0:512],
                            bq_sb[:, o:o + 1], QSC, ALU.add, ALU.mult),
                        [tlast], "vm")
                bank_free[bnk] = te
                wsl_busy[sl] = te

        if DEBUG:
            dma("gpsimd", dbg["dbg_q"][:, :], qbf[:, :], waits=[("vm", B.cnt["vm"])])
        tq_ld = dma("sync", qrow_sb[0:1, :], qrow[0:1, :])
        qmi_toks = []
        for hh in range(2):
            tmm = mmop(2, pb[2][:, 0:512], ones_f[0:1, 0:128], qrow_sb[0:1, hh * 512:(hh + 1) * 512],
                       True, True, waits=[tq_ld, t_c7], inc=True)
            tev = op("vector",
                     lambda e, h=hh: e.tensor_scalar(qmi[:, h * 512:(h + 1) * 512], pb[2][:, 0:512],
                                                     iota_sb[:, 0:1], None, ALU.subtract),
                     [tmm], "vm")
            bank_free[2] = tev
            qmi_toks.append(tev)

        exp_free = [None, None]
        sbank_free = [None, None]
        PVB, SUMB, RBB = 2, 3, 6
        for g in range(4):
            nsc = 16 * (g + 1)
            mask_toks = []
            for k in range(16):
                thr = float(128 * (16 * g + k))
                tm_ = op("vector",
                         lambda e, kk=k, th=thr, gg=g: e.tensor_scalar(
                             mask_sb[:, kk * 256:(kk + 1) * 256], qmi[:, gg * 256:(gg + 1) * 256],
                             th, None, ALU.is_ge),
                         qmi_toks if k == 0 else [], "vm")
                mask_toks.append(tm_)
            for h in range(16):
                kv = h // 8
                npair = nsc // 2
                pend = {}

                def emit_S(p):
                    for half in range(2):
                        sc = 2 * p + half
                        bnk = p % 2
                        w = []
                        if half == 0 and sbank_free[bnk] is not None:
                            w.append(sbank_free[bnk])
                        if g == 0 and h == 0 and p < 2 and half == 0:
                            w.append(("vm", B.cnt["vm"]))
                        mmop(bnk, pb[bnk][:, half * 256:half * 256 + 256],
                             kbf[:, kv * 8192 + sc * 128:kv * 8192 + (sc + 1) * 128],
                             qbf[:, h * 1024 + g * 256:h * 1024 + (g + 1) * 256],
                             True, True, waits=w, inc=(half == 1))
                    return ("tm", B.cnt["tm"])

                tS = emit_S(0)
                for p in range(npair):
                    ebuf = exp0 if p % 2 == 0 else exp1
                    bnk = p % 2
                    w = [tS]
                    if exp_free[p % 2] is not None:
                        w.append(exp_free[p % 2])
                    texp = op("scalar",
                              lambda e, eb=ebuf, b=bnk: e.activation(eb[:, :], pb[b][:, 0:512], AF.Exp),
                              w, "sm")
                    sbank_free[bnk] = texp
                    dep = [texp]
                    for half in range(2):
                        sc = 2 * p + half
                        if sc >= 16 * g:
                            k = sc - 16 * g
                            tmk = op("vector",
                                     lambda e, eb=ebuf, hf=half, kk=k: e.tensor_mul(
                                         eb[:, hf * 256:hf * 256 + 256], eb[:, hf * 256:hf * 256 + 256],
                                         mask_sb[:, kk * 256:(kk + 1) * 256]),
                                     [texp, mask_toks[k]], "vm")
                            dep.append(tmk)
                    if p + 1 < npair:
                        tS = emit_S(p + 1)
                    for half in range(2):
                        sc = 2 * p + half
                        wS = list(dep) if half == 0 else []
                        if sc == 0 and bank_free[SUMB] is not None:
                            wS.append(bank_free[SUMB])
                        mmop(SUMB, pb[SUMB][0:1, 0:256], ones_bf[:, 0:1],
                             ebuf[:, half * 256:half * 256 + 256],
                             sc == 0, sc == nsc - 1, waits=wS, inc=(sc == nsc - 1))
                        if sc == nsc - 1:
                            pend["sum"] = ("tm", B.cnt["tm"])
                        wP = []
                        if sc == 0 and bank_free[PVB] is not None:
                            wP.append(bank_free[PVB])
                        mmop(PVB, pb[PVB][:, 0:256],
                             vbf[:, sc * 256 + kv * 128:sc * 256 + kv * 128 + 128],
                             ebuf[:, half * 256:half * 256 + 256],
                             sc == 0, sc == nsc - 1, waits=wP, inc=(sc == nsc - 1 or half == 1))
                        if sc == nsc - 1:
                            pend["pv"] = ("tm", B.cnt["tm"])
                    exp_free[p % 2] = ("tm", B.cnt["tm"])
                tr = op("vector", lambda e: e.reciprocal(rb_sb[0:1, 0:256], pb[SUMB][0:1, 0:256]),
                        [pend["sum"]], "vm")
                bank_free[SUMB] = tr
                tbc = mmop(RBB, pb[RBB][:, 0:256], ones_f[0:1, 0:128], rb_sb[0:1, 0:256], True, True,
                           waits=[tr], inc=True)
                tbe = op("vector", lambda e: e.tensor_copy(rb_sb[:, 0:256], pb[RBB][:, 0:256]), [tbc], "vm")
                bank_free[RBB] = tbe
                tev = op("vector",
                         lambda e, hh=h, gg=g: e.tensor_mul(
                             acat[:, hh * 1024 + gg * 256:hh * 1024 + (gg + 1) * 256],
                             pb[PVB][:, 0:256], rb_sb[:, 0:256]),
                         [tbe, pend["pv"]], "vm")
                bank_free[PVB] = tev
        attn_done = ("vm", B.cnt["vm"])
        if DEBUG:
            dma("gpsimd", dbg["dbg_acat"][:, :], acat[:, :], waits=[attn_done])

        wsl_busy = [wsl_busy[0], wsl_busy[1]]
        evac_map = {}
        for oc in range(16):
            sl = oc % 2
            w = [wbf_busy[sl]] if wbf_busy[sl] else []
            tld = dma("sync", bass.AP(wsl[sl], 0, [[2048, 128], [128, 16], [1, 128]]),
                      bass.AP(wo.tensor, oc * 128, [[2048, 128], [2048 * 128, 16], [1, 128]]),
                      waits=w, sem="d2" if sl == 0 else "d3")
            tc = op("vector", lambda e, s=sl: e.tensor_copy(wbf[s][:, :], wsl[s][:, :]),
                    [tld] + ([wsl_busy[sl]] if wsl_busy[sl] else []), "vm")
            wbf_busy[sl] = tc
            for qb in range(2):
                bnk = (oc * 2 + qb) % 2
                for ic in range(16):
                    mmop(bnk, pb[bnk][:, 0:512], wbf[sl][:, ic * 128:(ic + 1) * 128],
                         acat[:, ic * 1024 + qb * 512:ic * 1024 + qb * 512 + 512], ic == 0, ic == 15,
                         waits=([tc, attn_done] if ic == 0 else []), inc=(ic == 15))
                tlast = ("tm", B.cnt["tm"])
                slot = aobf[:, oc * 1024 + qb * 512:oc * 1024 + qb * 512 + 512]
                te = op("vector",
                        lambda e, b=bnk, o=oc, s=slot: e.tensor_scalar(s, pb[b][:, 0:512],
                                                                       bo_sb[:, o:o + 1], None, ALU.add),
                        [tlast], "vm")
                bank_free[bnk] = te
                wsl_busy[sl] = te
                evac_map[(oc, qb)] = te

        x1_last_st = None
        for qb in range(2):
            stat_tok = None
            for dc in range(16):
                slot = aobf[:, dc * 1024 + qb * 512:dc * 1024 + qb * 512 + 512]
                tsq = op("scalar", lambda e, s=slot: e.activation(xsq[:, :], s, AF.Square),
                         [evac_map[(dc, qb)]] + ([prev_tbq] if prev_tbq else []), "sm")
                sw = [tsq, evac_map[(dc, qb)]]
                if dc == 0:
                    sw += [bank_free[sA], bank_free[sB]]
                ta, tbq = ln_stats_mm(slot, xsq[:, :], dc == 0, dc == 15, waits=sw)
                prev_tbq = tbq
                if dc == 15:
                    stat_tok = [ta, tbq]
            fin = ln_finalize(stat_tok)
            last_v = fin
            last_st = None
            for dc in range(16):
                slot = aobf[:, dc * 1024 + qb * 512:dc * 1024 + qb * 512 + 512]
                tld = dma("gpsimd", xt0[:, :], xownT[dc * 128:(dc + 1) * 128, qb * 512:(qb + 1) * 512],
                          waits=last_v, sem="g0")
                tmu = op("vector", lambda e, s=slot: e.tensor_mul(tscr[:, :], s, r_bf[:, :]),
                         last_v, "vm")
                tsu = op("vector", lambda e: e.tensor_sub(tscr[:, :], tscr[:, :], mr_bf[:, :]), [tmu], "vm")
                tad = op("vector", lambda e: e.tensor_add(xt1[:, :], tscr[:, :], xt0[:, :]),
                         [tsu, tld] + ([last_st] if last_st else []), "vm")
                tcp = op("vector",
                         lambda e, d=dc, q=qb: e.tensor_copy(
                             x1bf[:, d * 1024 + q * 512:d * 1024 + q * 512 + 512], xt1[:, :]),
                         [tad], "vm")
                tst = dma("gpsimd", x1s[dc * 128:(dc + 1) * 128, qb * 512:(qb + 1) * 512], xt1[:, :],
                          waits=[tcp], sem="g1")
                last_v = [tcp]
                last_st = tst
            x1_last_st = last_st

        for qb in range(2):
            stat_tok = None
            for dc in range(16):
                slot = x1bf[:, dc * 1024 + qb * 512:dc * 1024 + qb * 512 + 512]
                tsq = op("scalar", lambda e, s=slot: e.activation(xsq[:, :], s, AF.Square),
                         ([("vm", B.cnt["vm"])] if dc == 0 else []) + ([prev_tbq] if prev_tbq else []), "sm")
                sw = [tsq]
                if dc == 0:
                    sw += [bank_free[sA], bank_free[sB]]
                ta, tbq = ln_stats_mm(slot, xsq[:, :], dc == 0, dc == 15, waits=sw)
                prev_tbq = tbq
                if dc == 15:
                    stat_tok = [ta, tbq]
            fin = ln_finalize(stat_tok)
            last_v = fin
            for dc in range(16):
                src = x1bf[:, dc * 1024 + qb * 512:dc * 1024 + qb * 512 + 512]
                dst = x2n[:, dc * 1024 + qb * 512:dc * 1024 + qb * 512 + 512]
                tmu = op("vector", lambda e, s=src: e.tensor_mul(tscr[:, :], s, r_bf[:, :]), last_v, "vm")
                tsu = op("vector", lambda e, d=dst: e.tensor_sub(d, tscr[:, :], mr_bf[:, :]), [tmu], "vm")
                last_v = [tsu]
        x2n_done = ("vm", B.cnt["vm"])
        if DEBUG:
            dma("gpsimd", dbg["dbg_x1"][:, :], x1bf[:, :], waits=[x2n_done])
            dma("gpsimd", dbg["dbg_x2n"][:, :], x2n[:, :], waits=[x2n_done])

        red_tok = [None, None]
        gfin = []
        for wi, wt in enumerate((w1, w2, w3)):
            for i in range(44):
                sl = i % 2
                src = bass.AP(wt.tensor, i * 262144, [[2048, 128], [1, 2048]])
                w = [red_tok[sl]] if red_tok[sl] else []
                if wi == 0 and i < 2:
                    w += [x2n_done, wsl_busy[sl]]
                tld = dma("sync", wsl[sl][:, :], src, waits=w, sem="d2" if sl == 0 else "d3")
                trd = op("vector",
                         lambda e, s=sl, ii=i: e.tensor_reduce(gacc[:, ii:ii + 1], wsl[s][:, :], AX.X,
                                                               ALU.add, apply_absolute_value=True),
                         [tld], "vm")
                red_tok[sl] = trd
            tsum = op("vector", lambda e: e.tensor_reduce(gth[:, 6:7], gacc[:, 0:44], AX.X, ALU.add),
                      [red_tok[0], red_tok[1]], "vm")
            tmm = mmop(7, pb[7][:, 0:1], ones_f[:, :], gth[:, 6:7], True, True, waits=[tsum], inc=True)
            tth = op("vector",
                     lambda e, wii=wi: e.tensor_scalar(gth[:, 2 * wii:2 * wii + 1], pb[7][:, 0:1],
                                                       0.5 / NW, 0.5 * EPS_Q, ALU.mult, ALU.add),
                     [tmm], "vm")
            bank_free[7] = tth
            tng = op("vector",
                     lambda e, wii=wi: e.tensor_scalar(gth[:, 2 * wii + 1:2 * wii + 2],
                                                       gth[:, 2 * wii:2 * wii + 1], -1.0, None, ALU.mult),
                     [tth], "vm")
            gfin.append(tng)
            red_tok = [tng, tng]

        def trans_quant(src_sl, wi, dst_fn, first_waits, consumer_tok, last_q0):
            last_q = last_q0
            for wave in range(4):
                bnk = 4 + wave % 2
                for j in range(4):
                    dcq = wave * 4 + j
                    w = []
                    if j == 0:
                        w = list(first_waits) if wave == 0 else []
                        if bank_free[bnk] is not None:
                            w.append(bank_free[bnk])
                    op("tensor",
                       lambda e, b=bnk, s=src_sl, dd=dcq, jj=j: e.transpose(
                           pb[b][:, jj * 128:(jj + 1) * 128],
                           wsl[s][:, dd * 128:(dd + 1) * 128], identf[:, :]),
                       w, "tm" if j == 3 else None)
                ttr = ("tm", B.cnt["tm"])
                wb = [ttr, gfin[wi]] + ([last_q] if last_q else [])
                tb_ = op("vector",
                         lambda e, b=bnk, w_=wi: e.tensor_scalar(
                             wbf[1][:, 0:512], pb[b][:, 0:512],
                             gth[:, 2 * w_ + 1:2 * w_ + 2], None, ALU.is_le),
                         wb, "vm")
                wq_ = [tb_]
                if wave == 0 and consumer_tok is not None:
                    wq_.append(consumer_tok)
                tq_ = op("vector",
                         lambda e, b=bnk, w_=wi, wv=wave: e.scalar_tensor_tensor(
                             dst_fn(wv), pb[b][:, 0:512], gth[:, 2 * w_:2 * w_ + 1],
                             wbf[1][:, 0:512], ALU.is_ge, ALU.subtract),
                         wq_, "vm")
                bank_free[bnk] = tq_
                last_q = tq_
            return last_q, ("tm", B.cnt["tm"])

        h_banks = {("h1", 0): 0, ("h1", 1): 1, ("h2", 0): 2, ("h2", 1): 3}
        wsl_busy2 = [wsl_busy[0], wsl_busy[1]]
        tq_used = {("w1", 0): None, ("w1", 1): None, ("w2", 0): None, ("w2", 1): None}
        last_q = None
        w3st_busy = [None, None]
        fpar = [0]
        last_w3_dma = None
        for hc in range(44):
            par = hc % 2
            sl = hc % 2
            tld1 = dma("sync", wsl[sl][:, :], w1[hc * 128:(hc + 1) * 128, 0:2048],
                       waits=[wsl_busy2[sl]] if wsl_busy2[sl] else [], sem="d2" if sl == 0 else "d3")
            ev1, rd1 = trans_quant(sl, 0,
                                   lambda wv, p=par: tq1[p][:, wv * 512:(wv + 1) * 512],
                                   [tld1], tq_used[("w1", par)], last_q)
            last_q = ev1
            wsl_busy2[sl] = rd1
            sl2 = (hc + 1) % 2
            tld2 = dma("sync", wsl[sl2][:, :], w2[hc * 128:(hc + 1) * 128, 0:2048],
                       waits=[wsl_busy2[sl2]] if wsl_busy2[sl2] else [], sem="d2" if sl2 == 0 else "d3")
            ev2, rd2 = trans_quant(sl2, 1,
                                   lambda wv, p=par: tq2[p][:, wv * 512:(wv + 1) * 512],
                                   [tld2], tq_used[("w2", par)], last_q)
            last_q = ev2
            wsl_busy2[sl2] = rd2
            for which, tqbuf, ready in (("h1", tq1[par], ev1), ("h2", tq2[par], ev2)):
                for qb in range(2):
                    bnk = h_banks[(which, qb)]
                    for dc in range(16):
                        mmop(bnk, pb[bnk][:, 0:512], tqbuf[:, dc * 128:(dc + 1) * 128],
                             x2n[:, dc * 1024 + qb * 512:dc * 1024 + qb * 512 + 512], dc == 0, dc == 15,
                             waits=([ready, x2n_done] if dc == 0 else []), inc=(dc == 15))
            tmm_done = ("tm", B.cnt["tm"])
            tq_used[("w1", par)] = tmm_done
            tq_used[("w2", par)] = tmm_done
            for qb in range(2):
                sgb = tscr if qb == 0 else xsq
                tsg = op("scalar",
                         lambda e, b=h_banks[("h1", qb)], s=sgb: e.activation(s[:, :], pb[b][:, 0:512],
                                                                              AF.Sigmoid),
                         [tmm_done] + ([prev_tbq] if hc == 0 and qb == 0 and prev_tbq else []), "sm")
                thd = op("vector",
                         lambda e, b=h_banks[("h2", qb)], s=sgb, h=hc, q=qb: e.tensor_mul(
                             hid_slot(h, q), s[:, :], pb[b][:, 0:512]),
                         [tsg], "vm")
                bank_free[h_banks[("h1", qb)]] = thd
                bank_free[h_banks[("h2", qb)]] = thd
            sl3 = hc % 2
            tld3 = dma("sync", bass.AP(wsl[sl3], 0, [[2048, 128], [128, 16], [1, 128]]),
                       bass.AP(w3.tensor, hc * 128, [[5632, 128], [5632 * 128, 16], [1, 128]]),
                       waits=[wsl_busy2[sl3]] if wsl_busy2[sl3] else [], sem="d2" if sl3 == 0 else "d3")
            for wave in range(4):
                bnk = 4 + wave % 2
                for j in range(4):
                    dcq = wave * 4 + j
                    w = []
                    if j == 0:
                        w = [tld3] if wave == 0 else []
                        if bank_free[bnk] is not None:
                            w.append(bank_free[bnk])
                    op("tensor",
                       lambda e, b=bnk, s=sl3, dd=dcq, jj=j: e.transpose(
                           pb[b][:, jj * 128:(jj + 1) * 128],
                           wsl[s][:, dd * 128:(dd + 1) * 128], identf[:, :]),
                       w, "tm" if j == 3 else None)
                ttr = ("tm", B.cnt["tm"])
                tb_ = op("vector",
                         lambda e, b=bnk: e.tensor_scalar(
                             wbf[1][:, 0:512], pb[b][:, 0:512], gth[:, 5:6], None, ALU.is_le),
                         [ttr, gfin[2]] + ([last_q] if last_q else []), "vm")
                fslot = fpar[0] % 2
                fb = fst0 if fslot == 0 else fst1
                fpar[0] += 1
                wq_ = [tb_] + ([w3st_busy[fslot]] if w3st_busy[fslot] else [])
                tq_ = op("vector",
                         lambda e, b=bnk, f=fb: e.scalar_tensor_tensor(
                             f[:, 0:512], pb[b][:, 0:512], gth[:, 4:5],
                             wbf[1][:, 0:512], ALU.is_ge, ALU.subtract),
                         wq_, "vm")
                bank_free[bnk] = tq_
                last_q = tq_
                tdm = dma("gpsimd",
                          bass.AP(w3qts.tensor, (hc * 128) * D + wave * 512, [[D, 128], [1, 512]]),
                          fb[:, 0:512], waits=[tq_], sem="g1" if fslot == 0 else "g2")
                w3st_busy[fslot] = tdm
                last_w3_dma = tdm
            wsl_busy2[sl3] = ("tm", B.cnt["tm"])
        hid_done = ("vm", B.cnt["vm"])
        p7a_tm = ("tm", B.cnt["tm"])

        w3slot_mm = [None] * 4
        f_toks = []
        fstp = [0]
        for qq in range(4):
            qb, half = qq // 2, qq % 2
            for hc in range(44):
                sl = hc % 4
                w = [w3slot_mm[sl]] if w3slot_mm[sl] else []
                if qq == 0 and hc < 4:
                    w += [p7a_tm, w3st_busy[0], w3st_busy[1]]
                tld = dma("sync", w3sl[sl][:, :],
                          bass.AP(w3qts.tensor, (hc * 128) * D, [[D, 128], [1, 2048]]),
                          waits=w, sem=f"d{sl}")
                for dc in range(16):
                    bnk = dc // 2
                    colo = (dc % 2) * 256
                    wmm = []
                    if dc == 0:
                        wmm.append(tld)
                        if hc == 0:
                            wmm.append(hid_done)
                    if hc == 0 and bank_free[bnk] is not None:
                        wmm.append(bank_free[bnk])
                    op("tensor",
                       lambda e, b=bnk, co=colo, s=sl, d=dc, h=hc, q=qb, hf=half: e.matmul(
                           pb[b][:, co:co + 256], w3sl[s][:, d * 128:(d + 1) * 128],
                           hid_slot(h, q)[:, hf * 256:hf * 256 + 256],
                           start=(h == 0 and d % 2 == 0), stop=(h == 43)),
                       wmm, "tm" if dc == 15 else None)
                w3slot_mm[sl] = ("tm", B.cnt["tm"])
            tdone = ("tm", B.cnt["tm"])
            for dc in range(16):
                bnk = dc // 2
                colo = (dc % 2) * 256
                fb = fst0 if fstp[0] % 2 == 0 else fst1
                fslot = fstp[0] % 2
                fstp[0] += 1
                wev = [tdone] if dc == 0 else []
                if w3st_busy[fslot] is not None:
                    wev.append(w3st_busy[fslot])
                tev = op("scalar",
                         lambda e, b=bnk, co=colo, f=fb: e.copy(f[:, 0:256], pb[b][:, co:co + 256]),
                         wev, "sm")
                if dc % 2 == 1:
                    bank_free[bnk] = tev
                tdm = dma("gpsimd", fs[dc * 128:(dc + 1) * 128, qq * 256:(qq + 1) * 256], fb[:, 0:256],
                          waits=[tev], sem="g1" if fslot == 0 else "g2")
                w3st_busy[fslot] = tdm
                f_toks.append(tdm)

        fdone = [w3st_busy[0], w3st_busy[1]]
        if DEBUG:
            dma("gpsimd", dbg["dbg_hA"][:, :], hidA[:, :], waits=[hid_done])
            dma("gpsimd", dbg["dbg_hB"][:, :], hidB[:, :], waits=[hid_done])
            dma("gpsimd", dbg["dbg_hC"][:, :], hidC[:, :], waits=[hid_done])
            dma("gpsimd", dbg["dbg_gth"][:, :], gth[:, :], waits=[gfin[2]])
            dma("gpsimd", dbg["dbg_fs"][:, :], fs[:, :], waits=fdone)
        out_last_st = None
        fb_busy = [None, None]
        for qb in range(2):
            stat_tok = None
            for dc in range(16):
                sl = dc % 2
                fb = fst0 if sl == 0 else fst1
                w = [fb_busy[sl]] if fb_busy[sl] else []
                if dc < 2 and qb == 0:
                    w += fdone
                tld = dma("sync", fb[:, :], fs[dc * 128:(dc + 1) * 128, qb * 512:(qb + 1) * 512], waits=w,
                          sem="d0" if sl == 0 else "d1")
                tsq = op("scalar", lambda e, f=fb: e.activation(xsq[:, :], f[:, :], AF.Square),
                         [tld] + ([prev_tbq] if prev_tbq else []), "sm")
                sw = [tsq]
                if dc == 0:
                    sw += [bank_free[sA], bank_free[sB]]
                ta, tbq = ln_stats_mm(fb[:, :], xsq[:, :], dc == 0, dc == 15, waits=sw)
                fb_busy[sl] = tbq
                prev_tbq = tbq
                if dc == 15:
                    stat_tok = [ta, tbq]
            fin = ln_finalize(stat_tok)
            last_v = fin
            for dc in range(16):
                sl = dc % 2
                fb = fst0 if sl == 0 else fst1
                tldf = dma("sync", fb[:, :], fs[dc * 128:(dc + 1) * 128, qb * 512:(qb + 1) * 512],
                           waits=([fb_busy[sl]] if fb_busy[sl] else []) + last_v,
                           sem="d0" if sl == 0 else "d1")
                tldx = dma("gpsimd", xt0[:, :], x1s[dc * 128:(dc + 1) * 128, qb * 512:(qb + 1) * 512],
                           waits=([out_last_st] if out_last_st else [x1_last_st]) + last_v)
                tmu = op("vector", lambda e, f=fb: e.tensor_mul(tscr[:, :], f[:, :], r_bf[:, :]),
                         [tldf] + last_v, "vm")
                tsu = op("vector", lambda e: e.tensor_sub(tscr[:, :], tscr[:, :], mr_bf[:, :]), [tmu], "vm")
                tad = op("vector", lambda e: e.tensor_add(m_sb[:, :], tscr[:, :], xt0[:, :]),
                         [tsu, tldx], "vm")
                tdm = dma("gpsimd", outT[dc * 128:(dc + 1) * 128, qb * 512:(qb + 1) * 512], m_sb[:, :],
                          waits=[tad], sem="g3")
                fb_busy[sl] = tmu
                out_last_st = tdm
                last_v = [tad]

        final_dma = {s: B.cnt[s] for s in SEM_DMA}

        with nc.Block() as block:

            def emit(eng_name):
                def inner(eng):
                    for waits, fn, inc in B.ops[eng_name]:
                        for s, v in waits:
                            eng.wait_ge(sems[s], v)
                        ins = fn(eng)
                        if inc is not None:
                            amt = 16 if inc in SEM_DMA else 1
                            ins.then_inc(sems[inc], amt)
                    if eng_name == "gpsimd":
                        for s_, v_ in final_dma.items():
                            if v_ > 0:
                                eng.wait_ge(sems[s_], v_)
                return inner

            block.sync(emit("sync"))
            block.gpsimd(emit("gpsimd"))
            block.tensor(emit("tensor"))
            block.vector(emit("vector"))
            block.scalar(emit("scalar"))

    return nc


def _own_tiles(c):
    tiles = []
    for g in range(4):
        tiles.append(16 * g + c)
        tiles.append(16 * g + 15 - c)
    return tiles


def kernel(L0_attn_x, L0_attn_Wq, L0_attn_bq, L0_attn_Wk, L0_attn_bk,
           L0_attn_Wv, L0_attn_bv, L0_attn_Wo, L0_attn_bo,
           L0_ffn_w1, L0_ffn_w2, L0_ffn_w3):
    x = np.ascontiguousarray(np.asarray(L0_attn_x, dtype=np.float32).reshape(T, D))
    xT = np.ascontiguousarray(x.T)
    ones_bf = np.ones((128, 128), dtype=BF)
    ones_f = np.ones((128, 128), dtype=np.float32)
    ident = np.eye(128, dtype=np.float32)
    iota = np.arange(128, dtype=np.float32).reshape(128, 1)

    shared = {
        "xT": xT,
        "wq": np.ascontiguousarray(np.asarray(L0_attn_Wq, dtype=np.float32)),
        "wk": np.ascontiguousarray(np.asarray(L0_attn_Wk, dtype=np.float32)),
        "wv": np.ascontiguousarray(np.asarray(L0_attn_Wv, dtype=np.float32)),
        "wo": np.ascontiguousarray(np.asarray(L0_attn_Wo, dtype=np.float32)),
        "bq": np.asarray(L0_attn_bq, dtype=np.float32),
        "bk": np.asarray(L0_attn_bk, dtype=np.float32),
        "bv": np.asarray(L0_attn_bv, dtype=np.float32),
        "bo": np.asarray(L0_attn_bo, dtype=np.float32),
        "w1": np.ascontiguousarray(np.asarray(L0_ffn_w1, dtype=np.float32)),
        "w2": np.ascontiguousarray(np.asarray(L0_ffn_w2, dtype=np.float32)),
        "w3": np.ascontiguousarray(np.asarray(L0_ffn_w3, dtype=np.float32)),
        "ones_bf": ones_bf, "ones_f": ones_f, "ident": ident, "iota": iota,
        "sconst": np.concatenate([np.full((128, 1), 1e-5, np.float32),
                                  np.full((128, 1), -0.5, np.float32)], axis=1),
    }

    in_maps = []
    rows_per_core = []
    for c in range(NC):
        tiles = _own_tiles(c)
        rows = np.concatenate([np.arange(128 * t, 128 * t + 128) for t in tiles])
        rows_per_core.append(rows)
        m = dict(shared)
        m["xownT"] = np.ascontiguousarray(xT[:, rows])
        m["qrow"] = rows.astype(np.float32).reshape(1, OWN)
        in_maps.append(m)

    nc = build_nc()
    res = run_bass_kernel_spmd(nc, in_maps, core_ids=list(range(NC)))
    if DEBUG:
        global LAST_RES
        LAST_RES = res

    out = np.empty((T, D), dtype=np.float32)
    for c in range(NC):
        out[rows_per_core[c]] = np.asarray(res.results[c]["outT"]).T
    return out.reshape(1, T, D)


if __name__ == "__main__":
    build_nc()
    print("build ok")
